# version 14
# baseline (speedup 1.0000x reference)
import numpy as np
import ml_dtypes
import concourse.bass as bass
import concourse.mybir as mybir
from concourse.bass_utils import run_bass_kernel_spmd

F32 = mybir.dt.float32
BF16 = mybir.dt.bfloat16
F32R = mybir.dt.float32r
AF = mybir.ActivationFunctionType
ALU = mybir.AluOpType
bf16 = ml_dtypes.bfloat16

B, T, C = 4, 4096, 2048
H, DH = 16, 128
INTER = 5632
NCORES = 8
TCORE = 2048
HALO = 256
ASPAN = 512
FSPAN = 1024
NA = TCORE // ASPAN
NF = TCORE // FSPAN
NJ = C // 128
NI = INTER // 128
IGRP = 22
SHIFT = 14.0
SCALE = DH ** -0.5
NEG = -1.0e30
EPS = 1e-5

CV_SHIFT, CV_SCALE, CV_ZERO, CV_EPS, CV_NEGH, CV_ONE, CV_NEG1 = range(7)


class _Tok:
    def __init__(self, node):
        self.node = node

    def then_inc(self, s, n):
        self.node[3].append((s, n))
        return self


class _Eng:
    def __init__(self, stream):
        self._stream = stream

    def __getattr__(self, op):
        st = self._stream

        def call(*a, **k):
            node = (op, a, k, [])
            st.append(node)
            return _Tok(node)
        return call


def _replay(eng, stream):
    for op, a, k, incs in stream:
        ins = getattr(eng, op)(*a, **k)
        for s, n in incs:
            ins.then_inc(s, n)


class _Ctr:
    def __init__(self):
        self.v = {}

    def inc(self, name, k=1):
        self.v[name] = self.v.get(name, 0) + k
        return self.v[name]

    def val(self, name):
        return self.v.get(name, 0)


def build_full(nphases=6, debug=()):
    nc = bass.Bass("TRN2", target_bir_lowering=False, detect_race_conditions=False)

    dt = nc.dram_tensor
    xT = dt("xT", [C, HALO + TCORE], F32, kind="ExternalInput").ap()
    wqS = dt("wqS", [NJ, 128, C], F32, kind="ExternalInput").ap()
    wkS = dt("wkS", [NJ, 128, C], F32, kind="ExternalInput").ap()
    wvT = dt("wvT", [C, C], F32, kind="ExternalInput").ap()
    woS = dt("woS", [NJ, 128, C], F32, kind="ExternalInput").ap()
    gwS = dt("gwS", [NI, 128, C], F32, kind="ExternalInput").ap()
    uwS = dt("uwS", [NI, 128, C], F32, kind="ExternalInput").ap()
    dwS = dt("dwS", [NJ, 128, INTER], F32, kind="ExternalInput").ap()
    gs_d = dt("gsd", [128, NI], F32, kind="ExternalInput").ap()
    us_d = dt("usd", [128, NI], F32, kind="ExternalInput").ap()
    ds_d = dt("dsd", [128, NJ], F32, kind="ExternalInput").ap()
    trigC_d = dt("trigCd", [128, HALO + TCORE], BF16, kind="ExternalInput").ap()
    trigS_d = dt("trigSd", [128, HALO + TCORE], BF16, kind="ExternalInput").ap()
    tril_d = dt("trild", [128, 128], F32, kind="ExternalInput").ap()
    onesS_d = dt("onesSd", [128, 128], F32, kind="ExternalInput").ap()
    onesB_d = dt("onesBd", [128, 128], BF16, kind="ExternalInput").ap()
    onesB2_d = dt("onesB2d", [128, 128], BF16, kind="ExternalInput").ap()
    dgneg_d = dt("dgnegd", [128, 128], BF16, kind="ExternalInput").ap()
    ultri_d = dt("ultrid", [128, 128], BF16, kind="ExternalInput").ap()
    ultri2_d = dt("ultri2d", [128, 256], BF16, kind="ExternalInput").ap()
    cvec_d = dt("cvecd", [128, 8], F32, kind="ExternalInput").ap()
    bfirst_d = dt("bfirstd", [128, 1], F32, kind="ExternalInput").ap()
    yT = dt("yT", [C, TCORE], F32, kind="ExternalOutput").ap()

    al = nc.alloc_sbuf_tensor
    trigC = al("trigC", [128, HALO + TCORE], BF16)
    trigS = al("trigS", [128, HALO + TCORE], BF16)
    tril = al("tril", [128, 128], F32)
    onesS = al("onesS", [128, 128], F32)
    onesB = al("onesB", [128, 128], BF16)
    onesB2 = al("onesB2", [128, 128], BF16)
    dgneg = al("dgneg", [128, 128], BF16)
    ultri = al("ultri", [128, 128], BF16)
    ultri2 = al("ultri2", [128, 256], BF16)
    cvec = al("cvec", [128, 8], F32)
    bfirst = al("bfirst", [128, 1], F32)
    gs = al("gs", [128, NI], F32)
    us = al("us", [128, NI], F32)
    ds = al("ds", [128, NJ], F32)
    X = [al(f"X{j}", [128, FSPAN], F32) for j in range(NJ)]

    base = (nc.sbuf_base + 63) & ~63

    def at(name, shape, dtype, off):
        return nc.alloc_sbuf_tensor_at(name, shape, dtype, offset=base + off)

    Ktail = [at(f"Ktail{h}", [128, 256], BF16, h * 512) for h in range(H)]
    Vtail = [at(f"Vtail{i}", [128, 2048], BF16, 8192 + i * 4096) for i in range(2)]
    VOL = 16384
    xln1 = [at(f"xln1_{j}", [128, ASPAN], BF16, VOL + j * 1024) for j in range(NJ)]
    attn = [at(f"attn{j}", [128, ASPAN], BF16, VOL + j * 1024) for j in range(NJ)]
    Q = [at(f"Q{h}", [128, ASPAN], BF16, VOL + 16384 + h * 1024) for h in range(H)]
    Kc = [at(f"Kc{h}", [128, ASPAN], BF16, VOL + 32768 + h * 1024) for h in range(H)]
    Vc = [at(f"Vc{i}", [128, 2048], BF16, VOL + 49152 + i * 4096) for i in range(4)]
    wbuf = [at(f"wbuf{r}", [128, 2048], BF16, VOL + 65536 + r * 4096) for r in range(3)]
    sqkD = [at(f"sqkD{r}", [128, 4096], BF16, VOL + 77824 + r * 8192) for r in range(2)]
    sqk = [sqkD[r // 2][:, (r % 2) * 2048:(r % 2) * 2048 + 2048] for r in range(4)]
    WW = VOL + 94208
    xbb = [at(f"xbb{r}", [128, ASPAN], BF16, WW + r * 2048) for r in range(2)]
    xqb = [at(f"xqb{r}", [128, ASPAN], BF16, WW + r * 2048 + 1024) for r in range(2)]
    Msb = at("Msb", [128, ASPAN], F32, WW + 0)
    Rst = at("Rst", [128, ASPAN], F32, WW + 4096)
    lntmp = at("lntmp", [128, ASPAN], F32, WW + 6144)
    probsT = [at(f"probsT{r}", [128, 1024], BF16, WW + r * 2048) for r in range(2)]
    Rb = [at(f"Rb{r}", [128, 256], F32, WW + 4096 + r * 1024) for r in range(2)]
    ropet = [at(f"ropet{r}", [128, ASPAN], BF16, WW + r * 1024) for r in range(6)]
    Xh = [at(f"Xh{j}", [128, 256], BF16, VOL + 16384 + j * 512) for j in range(NJ)]
    A_TOP = VOL + 102400
    xln2 = [at(f"xln2_{j}", [128, FSPAN], BF16, VOL + j * 2048) for j in range(NJ)]
    hbuf = [at(f"h{i}", [128, FSPAN], BF16, VOL + 32768 + i * 2048) for i in range(IGRP)]
    RstF = at("RstF", [128, FSPAN], F32, VOL + 32768)
    lntmpF = at("lntmpF", [128, FSPAN], F32, VOL + 36864)
    gwb = [at(f"gwb{r}", [128, 2048], BF16, VOL + 77824 + r * 4096) for r in range(2)]
    uwb = [at(f"uwb{r}", [128, 2048], BF16, VOL + 86016 + r * 4096) for r in range(2)]
    dwb = [at(f"dwb{r}", [128, (IGRP // 2) * 128], BF16, VOL + 94208 + r * 2816) for r in range(2)]
    xbbF = [at(f"xbbF{r}", [128, 512], BF16, VOL + 99840 + r * 2048) for r in range(2)]
    xqbF = [at(f"xqbF{r}", [128, 512], BF16, VOL + 99840 + r * 2048 + 1024) for r in range(2)]
    MsbF = at("MsbF", [128, FSPAN], F32, VOL + 99840)
    silu_t = [at(f"silu{r}", [128, FSPAN], BF16, VOL + 103936 + r * 2048) for r in range(2)]
    F_TOP = VOL + 108032
    assert base + max(A_TOP, F_TOP) <= 229376, (base, A_TOP, F_TOP, nc.sbuf_top)

    PS = nc.alloc_psum_tensor("PS", [128, 4096], F32)

    def bk(b, w=512, off=0):
        return PS[:, b * 512 + off: b * 512 + off + w]

    sem_names = (["sW", "sX", "sT", "sV", "sS", "sO", "sCN", "sXH",
                  "sXa", "sXb", "cWK", "cWV", "cWQ", "cWO", "cGU", "cDW"]
                 + [f"qk{r}" for r in range(2)] + [f"wb{r}" for r in range(3)]
                 + [f"gw{r}" for r in range(2)] + [f"uw{r}" for r in range(2)]
                 + [f"dw{r}" for r in range(2)])
    sem_cms = {n: nc.semaphore(n) for n in sem_names}
    sem = {n: cm.__enter__() for n, cm in sem_cms.items()}

    c = _Ctr()
    streams = {n: [] for n in ["sync", "gpsimd", "tensor", "vector", "scalar"]}
    SY = _Eng(streams["sync"])
    G = _Eng(streams["gpsimd"])
    TE = _Eng(streams["tensor"])
    V = _Eng(streams["vector"])
    S = _Eng(streams["scalar"])

    def cv(i):
        return cvec[:, i:i + 1]

    ring_qk = [0, 0]
    ring_wv = [0, 0, 0]
    ring_gw = [0, 0]
    ring_uw = [0, 0]
    ring_dw = [0, 0]
    xsq_mark = [0, 0]
    xsqF_mark = [0, 0]
    cmark = {}

    for dst, src in [(trigC, trigC_d), (trigS, trigS_d), (tril, tril_d),
                     (onesS, onesS_d), (onesB, onesB_d), (onesB2, onesB2_d), (dgneg, dgneg_d), (ultri, ultri_d), (ultri2, ultri2_d), (cvec, cvec_d),
                     (bfirst, bfirst_d), (gs, gs_d), (us, us_d), (ds, ds_d)]:
        SY.dma_start(out=dst[:], in_=src).then_inc(sem["sCN"], 16)
        c.inc("sCN", 16)
    CONSTS_X = c.val("sCN")
    for E in (TE, V, S):
        E.wait_ge(sem["sCN"], CONSTS_X)

    def barrier():
        return {"sT": c.val("sT"), "sV": c.val("sV"), "sS": c.val("sS"),
                "sW": c.val("sW"), "sX": c.val("sX"), "sO": c.val("sO")}

    def wait_barrier(E, bm, *names):
        for n in names:
            if bm[n] > 0:
                E.wait_ge(sem[n], bm[n])

    def emit_ln(span, x_ap, xb_tiles, xq_tiles, xq_marks, xout, Rtile, tmptile, msb, pre_wait):
        nh = (span + 511) // 512
        b_m, b_sq, b_var = 0, nh, 2 * nh
        sq_t = []
        sq_s = {}
        for j in range(NJ):
            if j == 0:
                for sname, sval in pre_wait.get("V", []):
                    V.wait_ge(sem[sname], sval)
            for hh in range(nh):
                k = j * nh + hh
                slot = k % 2
                w = min(512, span - hh * 512)
                if xq_marks[slot] > 0:
                    V.wait_ge(sem["sT"], xq_marks[slot])
                V.tensor_copy(xb_tiles[slot][:, 0:w],
                              x_ap(j)[:, hh * 512: hh * 512 + w]).then_inc(sem["sV"], 1)
                cast_m = c.inc("sV")
                if k == 0:
                    for sname, sval in pre_wait.get("S", []):
                        S.wait_ge(sem[sname], sval)
                S.wait_ge(sem["sV"], cast_m)
                S.square(xq_tiles[slot][:, 0:w],
                         xb_tiles[slot][:, 0:w]).then_inc(sem["sS"], 1)
                sq_s[k] = c.inc("sS")
            if j == 0:
                for sname, sval in pre_wait.get("T", []):
                    TE.wait_ge(sem[sname], sval)
            for hh in range(nh):
                k = j * nh + hh
                slot = k % 2
                w = min(512, span - hh * 512)
                TE.wait_ge(sem["sS"], sq_s[k])
                TE.matmul(bk(b_m + hh, w), onesB2[:], xb_tiles[slot][:, 0:w],
                          start=(j == 0), stop=(j == NJ - 1))
                TE.matmul(bk(b_sq + hh, w), onesB2[:], xq_tiles[slot][:, 0:w],
                          start=(j == 0), stop=(j == NJ - 1)).then_inc(sem["sT"], 1)
                xq_marks[slot] = c.inc("sT")
            sq_t.append(xq_marks[(j * nh + nh - 1) % 2])
        V.wait_ge(sem["sT"], sq_t[-1])
        V.tensor_copy(msb[:, 0:span], PS[:, b_m * 512: b_m * 512 + span])
        V.scalar_tensor_tensor(tmptile[:, 0:span],
                               PS[:, b_m * 512: b_m * 512 + span], cv(CV_NEG1),
                               msb[:, 0:span], ALU.mult, ALU.mult)
        V.tensor_add(PS[:, b_var * 512: b_var * 512 + span],
                     PS[:, b_sq * 512: b_sq * 512 + span],
                     tmptile[:, 0:span]).then_inc(sem["sV"], 1)
        var_m = c.inc("sV")
        S.wait_ge(sem["sV"], var_m)
        S.activation(tmptile[:, 0:span], PS[:, b_var * 512: b_var * 512 + span],
                     AF.Ln, bias=cv(CV_EPS), scale=cv(CV_ONE))
        S.activation(Rtile[:, 0:span], tmptile[:, 0:span], AF.Exp,
                     bias=cv(CV_ZERO), scale=cv(CV_NEGH)).then_inc(sem["sS"], 1)
        r_m = c.inc("sS")
        V.wait_ge(sem["sS"], r_m)
        for j in range(NJ):
            V.tensor_sub(tmptile[:, 0:span], x_ap(j), msb[:, 0:span])
            ins = V.tensor_mul(xout(j), tmptile[:, 0:span], Rtile[:, 0:span])
        ins.then_inc(sem["sV"], 1)
        return c.inc("sV"), b_m

    def emit_qk_proj(which, strips_pair_d, scratch, dst, trig0, span, xsrc, ln_mark, bm):
        pair_v = [0] * (H // 2)
        str_t = [0] * NJ
        str_w = [0] * NJ
        for j in range(NJ):
            p = j // 2
            if j % 2 == 0:
                dslot = p % 2
                sl = f"qk{dslot}"
                G.wait_ge(sem["sT"], ring_qk[dslot]) if ring_qk[dslot] else None
                G.dma_start(out=sqkD[dslot][:].rearrange("p (s m) -> p s m", s=2),
                            in_=strips_pair_d[p]).then_inc(sem[sl], 16)
                str_w[j] = str_w[min(j + 1, NJ - 1)] = (sl, c.inc(sl, 16))
            slot = (p % 2) * 2 + (j % 2)
            pb = p % 2
            b = (j % 2) + 2 * pb
            if j == 0:
                TE.wait_ge(sem["sV"], ln_mark)
                wait_barrier(TE, bm, "sV")
            TE.wait_ge(sem[str_w[j][0]], str_w[j][1])
            if p >= 2 and j % 2 == 0:
                TE.wait_ge(sem["sV"], pair_v[p - 2])
            ins = None
            for i in range(NJ):
                ins = TE.matmul(bk(b, span), sqk[slot][:, i * 128:(i + 1) * 128],
                                xsrc[i][:, 0:span], start=(i == 0), stop=(i == NJ - 1))
            ins.then_inc(sem["sT"], 1)
            str_t[j] = c.inc("sT")
            if j % 2 == 1:
                ring_qk[p % 2] = str_t[j]
            if j % 2 == 1:
                V.wait_ge(sem["sT"], str_t[j])
                bR, bI = 2 * pb, 2 * pb + 1
                tc_ = trigC[:, trig0:trig0 + span]
                ts_ = trigS[:, trig0:trig0 + span]
                qrb, qib = ropet[4], ropet[5]
                V.tensor_copy(qrb[:, 0:span], bk(bR, span))
                V.tensor_copy(qib[:, 0:span], bk(bI, span))
                V.tensor_mul(ropet[0][:, 0:span], qrb[:, 0:span], tc_)
                V.tensor_mul(ropet[1][:, 0:span], qib[:, 0:span], ts_)
                V.tensor_mul(ropet[2][:, 0:span], qrb[:, 0:span], ts_)
                V.tensor_mul(ropet[3][:, 0:span], qib[:, 0:span], tc_)
                d0, d1 = dst[2 * p], dst[2 * p + 1]
                V.tensor_sub(d0[0:64, 0:span], ropet[0][0:64, 0:span], ropet[1][0:64, 0:span])
                V.tensor_sub(d1[0:64, 0:span], ropet[0][64:128, 0:span], ropet[1][64:128, 0:span])
                V.tensor_add(d0[64:128, 0:span], ropet[2][0:64, 0:span], ropet[3][0:64, 0:span])
                V.tensor_add(d1[64:128, 0:span], ropet[2][64:128, 0:span],
                             ropet[3][64:128, 0:span]).then_inc(sem["sV"], 1)
                pair_v[p] = c.inc("sV")
        return pair_v[-1]

    def emit_v_proj(ntj, xsrc, dstV, prev_v_mark):
        ev_last = prev_v_mark
        for dhalf in range(2):
            w_m = [0] * (NJ // 2)
            for ip in range(NJ // 2):
                slot = ip % 3
                sl = f"wb{slot}"
                G.wait_ge(sem["sT"], ring_wv[slot]) if ring_wv[slot] else None
                srcv = wvT[ip * 256: ip * 256 + 256,
                           dhalf * 1024: dhalf * 1024 + 1024]
                srcv = srcv.rearrange("(s p) m -> p s m", p=128)
                G.dma_start(out=wbuf[slot][:].rearrange("p (s m) -> p s m", s=2),
                            in_=srcv).then_inc(sem[sl], 16)
                w_m[ip] = (sl, c.inc(sl, 16))
                for ii in range(2):
                    i = 2 * ip + ii
                    if i == 0:
                        if isinstance(ev_last, tuple):
                            TE.wait_ge(sem[ev_last[0]], ev_last[1])
                        else:
                            TE.wait_ge(sem["sV"], ev_last)
                    if ii == 0:
                        TE.wait_ge(sem[w_m[ip][0]], w_m[ip][1])
                    ins = None
                    for tj in range(ntj):
                        for d2 in range(2):
                            ins = TE.matmul(
                                PS[:, tj * 1024 + d2 * 512: tj * 1024 + d2 * 512 + 512],
                                xsrc[i][:, tj * 128: tj * 128 + 128],
                                wbuf[slot][:, ii * 1024 + d2 * 512: ii * 1024 + d2 * 512 + 512],
                                start=(i == 0), stop=(i == NJ - 1))
                    ins.then_inc(sem["sT"], 1)
                    ring_wv[slot] = c.inc("sT")
            S.wait_ge(sem["sT"], ring_wv[(NJ // 2 - 1) % 3])
            ins = None
            for tj in range(ntj):
                ins = S.copy(dstV[tj][:, dhalf * 1024: dhalf * 1024 + 1024],
                             PS[:, tj * 1024: (tj + 1) * 1024])
            ins.then_inc(sem["sS"], 1)
            ev_last = ("sS", c.inc("sS"))
        return ev_last

    bm = barrier()
    xh_w = 0
    for j in range(NJ):
        G.dma_start(out=Xh[j][:], in_=xT[j * 128:(j + 1) * 128, 0:HALO]).then_inc(sem["sXH"], 16)
        xh_w = c.inc("sXH", 16)
    hsq_v = []
    hsq_t = []
    for j in range(NJ):
        if j == 0:
            V.wait_ge(sem["sXH"], xh_w)
        if j >= 2:
            V.wait_ge(sem["sT"], hsq_t[j - 2])
        V.tensor_mul(xqb[j % 2][:, 0:HALO], Xh[j][:], Xh[j][:]).then_inc(sem["sV"], 1)
        hsq_v.append(c.inc("sV"))
        if j == 0:
            TE.wait_ge(sem["sXH"], xh_w)
        TE.matmul(bk(0, HALO), onesB2[:], Xh[j][:], start=(j == 0), stop=(j == NJ - 1))
        TE.wait_ge(sem["sV"], hsq_v[j])
        TE.matmul(bk(1, HALO), onesB2[:], xqb[j % 2][:, 0:HALO],
                  start=(j == 0), stop=(j == NJ - 1)).then_inc(sem["sT"], 1)
        hsq_t.append(c.inc("sT"))
        xsq_mark[j % 2] = hsq_t[-1]
    V.wait_ge(sem["sT"], hsq_t[-1])
    V.tensor_copy(Msb[:, 0:HALO], bk(0, HALO))
    V.scalar_tensor_tensor(lntmp[:, 0:HALO], bk(0, HALO), cv(CV_NEG1),
                           Msb[:, 0:HALO], ALU.mult, ALU.mult)
    V.tensor_add(bk(2, HALO), bk(1, HALO), lntmp[:, 0:HALO]).then_inc(sem["sV"], 1)
    h_var = c.inc("sV")
    S.wait_ge(sem["sV"], h_var)
    S.activation(lntmp[:, 0:HALO], bk(2, HALO), AF.Ln, bias=cv(CV_EPS), scale=cv(CV_ONE))
    S.activation(Rst[:, 0:HALO], lntmp[:, 0:HALO], AF.Exp,
                 bias=cv(CV_ZERO), scale=cv(CV_NEGH)).then_inc(sem["sS"], 1)
    h_r = c.inc("sS")
    V.wait_ge(sem["sS"], h_r)
    for j in range(NJ):
        V.tensor_sub(lntmp[:, 0:HALO], Xh[j][:], Msb[:, 0:HALO])
        ins = V.tensor_mul(xln1[j][:, 0:HALO], lntmp[:, 0:HALO], Rst[:, 0:HALO])
    ins.then_inc(sem["sV"], 1)
    h_ln = c.inc("sV")

    wk_pairs = [wkS[2 * p:2 * p + 2, :, :].rearrange("s p m -> p s m") for p in range(NJ // 2)]
    klast = emit_qk_proj("wk", wk_pairs, None, Ktail, 0, HALO, [xln1[j] for j in range(NJ)], h_ln, bm)
    vlast = emit_v_proj(2, [xln1[j] for j in range(NJ)], Vtail, klast)

    last_store = {}

    def bump_rings(bm):
        for r in (ring_qk, ring_wv, ring_gw, ring_uw, ring_dw, xsq_mark, xsqF_mark):
            for idx in range(len(r)):
                r[idx] = max(r[idx], bm["sT"])

    def emit_A(t_idx):
        nonlocal vlast
        bm = barrier()
        if t_idx == 2:
            bump_rings(bm)
        for r in (xsq_mark, xsqF_mark):
            for idx in range(len(r)):
                r[idx] = max(r[idx], bm["sT"])
        acol0 = ASPAN * t_idx
        xcol0 = ASPAN * (t_idx % 2)
        tg0 = HALO + acol0

        f_prev = t_idx // 2 - 1
        if f_prev in last_store:
            SY.wait_ge(sem["sO"], last_store[f_prev])
        xs = "sXa" if t_idx % 2 == 0 else "sXb"
        xw = 0
        for j in range(NJ):
            SY.dma_start(out=X[j][:, xcol0:xcol0 + ASPAN],
                         in_=xT[j * 128:(j + 1) * 128, tg0:tg0 + ASPAN]).then_inc(sem[xs], 16)
            xw = c.inc(xs, 16)

        pre = {"V": [(xs, xw)], "S": [(xs, xw)],
               "T": [(xs, xw), ("sV", bm["sV"]), ("sS", bm["sS"])]}
        ln_mark, _ = emit_ln(ASPAN, lambda j: X[j][:, xcol0:xcol0 + ASPAN],
                             xbb, xqb, xsq_mark, lambda j: xln1[j][:],
                             Rst, lntmp, Msb, pre)

        wq_pairs = [wqS[2 * p:2 * p + 2, :, :].rearrange("s p m -> p s m") for p in range(NJ // 2)]
        wk_pairs = [wkS[2 * p:2 * p + 2, :, :].rearrange("s p m -> p s m") for p in range(NJ // 2)]
        emit_qk_proj("wq", wq_pairs, None, Q, tg0, ASPAN, xln1, ln_mark, bm)
        kp = emit_qk_proj("wk", wk_pairs, None, Kc, tg0, ASPAN, xln1, ln_mark, bm)
        vlast = emit_v_proj(4, xln1, Vc, kp)
        V.memset(probsT[0][:, 768:896], 0.0)
        V.memset(probsT[1][:, 768:896], 0.0).then_inc(sem["sV"], 1)
        pad_m = c.inc("sV")

        def kblk(ch):
            if ch == 0:
                return [(Ktail, 0), (Ktail, 128), (Kc, 0), (Kc, 128)]
            return [(Kc, 0), (Kc, 128), (Kc, 256), (Kc, 384)]

        def vblk(ch):
            if ch == 0:
                return [Vtail[0], Vtail[1], Vc[0], Vc[1]]
            return [Vc[0], Vc[1], Vc[2], Vc[3]]

        NCH = 2 * H
        av_ev = [0] * NCH
        for ch in range(2):
            kb = kblk(ch)
            vb = vblk(ch)
            for h in range(H):
                n = ch * H + h
                par = n % 2
                sc_b, sm_b, av_b = 4 * par, 4 * par + 2, 4 * par + 3
                if n == 0:
                    TE.wait_ge(sem[vlast[0]], vlast[1])
                    TE.wait_ge(sem["sV"], pad_m)
                if n >= 2:
                    TE.wait_ge(sem["sV"], av_ev[n - 2])
                qmv = Q[h][:, ch * 256: ch * 256 + 256]
                for b in range(3):
                    tile, col = kb[b]
                    TE.matmul(PS[:, sc_b * 512 + b * 256: sc_b * 512 + b * 256 + 256],
                              tile[h][:, col:col + 128], qmv, start=True, stop=True)
                TE.matmul(PS[:, sc_b * 512 + 512: sc_b * 512 + 768],
                          dgneg[:], ultri2[:], start=False, stop=True)
                tile, col = kb[3]
                TE.matmul(PS[:, sc_b * 512 + 896: sc_b * 512 + 1024],
                          tile[h][:, col:col + 128],
                          Q[h][:, ch * 256 + 128: ch * 256 + 256],
                          start=True, stop=False)
                TE.matmul(PS[:, sc_b * 512 + 896: sc_b * 512 + 1024],
                          dgneg[:], ultri[:], start=False, stop=True).then_inc(sem["sT"], 1)
                t_sc = c.inc("sT")
                S.wait_ge(sem["sT"], t_sc)
                pt = probsT[par]
                if t_idx == 0 and ch == 0:
                    S.activation(pt[:, 0:512], PS[:, sc_b * 512: sc_b * 512 + 512],
                                 AF.Exp, bias=bfirst[:], scale=cv(CV_SCALE))
                    S.activation(pt[:, 512:768], PS[:, sc_b * 512 + 512: sc_b * 512 + 768],
                                 AF.Exp, bias=cv(CV_SHIFT), scale=cv(CV_SCALE))
                else:
                    S.activation(pt[:, 0:768], PS[:, sc_b * 512: sc_b * 512 + 768],
                                 AF.Exp, bias=cv(CV_SHIFT), scale=cv(CV_SCALE))
                S.activation(pt[:, 896:1024], PS[:, sc_b * 512 + 896: sc_b * 512 + 1024],
                             AF.Exp, bias=cv(CV_SHIFT), scale=cv(CV_SCALE)).then_inc(sem["sS"], 1)
                s_exp = c.inc("sS")
                TE.wait_ge(sem["sS"], s_exp)
                for b in range(3):
                    TE.matmul(PS[:, sm_b * 512: sm_b * 512 + 256], onesB[:],
                              pt[:, b * 256:(b + 1) * 256], start=(b == 0), stop=False)
                TE.matmul(PS[:, sm_b * 512: sm_b * 512 + 256], onesB[:],
                          pt[:, 768:1024], start=False, stop=True)
                for b in range(3):
                    TE.matmul(PS[:, av_b * 512: av_b * 512 + 256],
                              vb[b][:, h * 128:(h + 1) * 128],
                              pt[:, b * 256:(b + 1) * 256], start=(b == 0), stop=False)
                TE.matmul(PS[:, av_b * 512: av_b * 512 + 256],
                          vb[3][:, h * 128:(h + 1) * 128],
                          pt[:, 768:1024], start=False, stop=True).then_inc(sem["sT"], 1)
                t_av = c.inc("sT")
                V.wait_ge(sem["sT"], t_av)
                V.reciprocal(Rb[par][:], PS[:, sm_b * 512: sm_b * 512 + 256])
                V.tensor_mul(attn[h][:, ch * 256: ch * 256 + 256],
                             PS[:, av_b * 512: av_b * 512 + 256],
                             Rb[par][:]).then_inc(sem["sV"], 1)
                av_ev[n] = c.inc("sV")

        wo_strips = [woS[j, :, :] for j in range(NJ)]
        o_w = [0] * NJ
        o_v = [0] * NJ
        for j in range(NJ):
            slot = j % 3
            sl = f"wb{slot}"
            G.wait_ge(sem["sT"], ring_wv[slot]) if ring_wv[slot] else None
            G.dma_start(out=wbuf[slot][:], in_=wo_strips[j]).then_inc(sem[sl], 16)
            o_w[j] = (sl, c.inc(sl, 16))
            b = j % 3
            if j == 0:
                TE.wait_ge(sem["sV"], av_ev[-1])
            TE.wait_ge(sem[o_w[j][0]], o_w[j][1])
            if j >= 3:
                TE.wait_ge(sem["sV"], o_v[j - 3])
            ins = None
            for i in range(NJ):
                ins = TE.matmul(bk(b), wbuf[slot][:, i * 128:(i + 1) * 128],
                                attn[i][:], start=(i == 0), stop=(i == NJ - 1))
            ins.then_inc(sem["sT"], 1)
            t_o = c.inc("sT")
            ring_wv[slot] = t_o
            V.wait_ge(sem["sT"], t_o)
            V.scalar_tensor_tensor(X[j][:, xcol0:xcol0 + ASPAN], bk(b), cv(CV_ONE),
                                   X[j][:, xcol0:xcol0 + ASPAN],
                                   ALU.mult, ALU.add).then_inc(sem["sV"], 1)
            o_v[j] = c.inc("sV")

        S.wait_ge(sem["sT"], c.val("sT"))
        for hh in range(H):
            S.copy(Ktail[hh][:], Kc[hh][:, 256:512])
        for i in range(2):
            ins = S.copy(Vtail[i][:], Vc[2 + i][:])
        ins.then_inc(sem["sS"], 1)
        c.inc("sS")

    def emit_F(f_idx):
        bm = barrier()
        bump_rings(bm)
        pre = {"V": [("sV", bm["sV"])], "S": [("sS", bm["sS"])],
               "T": [("sV", bm["sV"]), ("sS", bm["sS"])]}
        ln_mark, _ = emit_ln(FSPAN, lambda j: X[j][:], xbbF, xqbF, xsqF_mark,
                             lambda j: xln2[j][:], RstF, lntmpF, MsbF, pre)

        prev_group_v = 0
        st_marks = [0] * NJ
        for grp in range(2):
            i0 = grp * IGRP
            gu_w = [0] * IGRP
            h_v = [0] * IGRP
            for ii in range(IGRP):
                slot = ii % 2
                if grp == 0 and ii < 2:
                    wait_barrier(G, bm, "sT")
                G.wait_ge(sem["sT"], ring_gw[slot]) if ring_gw[slot] else None
                G.dma_start(out=gwb[slot][:], in_=gwS[i0 + ii, :, :]).then_inc(sem[f"gw{slot}"], 16)
                gm = (f"gw{slot}", c.inc(f"gw{slot}", 16))
                G.wait_ge(sem["sT"], ring_uw[slot]) if ring_uw[slot] else None
                G.dma_start(out=uwb[slot][:], in_=uwS[i0 + ii, :, :]).then_inc(sem[f"uw{slot}"], 16)
                gu_w[ii] = (f"uw{slot}", c.inc(f"uw{slot}", 16), gm)
                par = ii % 2
                bg, bu = 4 * par, 4 * par + 2
                if ii == 0:
                    TE.wait_ge(sem["sV"], ln_mark if grp == 0 else prev_group_v)
                TE.wait_ge(sem[gu_w[ii][0]], gu_w[ii][1])
                TE.wait_ge(sem[gu_w[ii][2][0]], gu_w[ii][2][1])
                if ii >= 2:
                    TE.wait_ge(sem["sV"], h_v[ii - 2])
                for i in range(NJ):
                    for hh2 in range(2):
                        TE.matmul(bk(bg + hh2), gwb[slot][:, i * 128:(i + 1) * 128],
                                  xln2[i][:, hh2 * 512: hh2 * 512 + 512],
                                  start=(i == 0), stop=(i == NJ - 1))
                ins = None
                for i in range(NJ):
                    for hh2 in range(2):
                        ins = TE.matmul(bk(bu + hh2), uwb[slot][:, i * 128:(i + 1) * 128],
                                        xln2[i][:, hh2 * 512: hh2 * 512 + 512],
                                        start=(i == 0), stop=(i == NJ - 1))
                ins.then_inc(sem["sT"], 1)
                t_gu = c.inc("sT")
                ring_gw[slot] = t_gu
                ring_uw[slot] = t_gu
                S.wait_ge(sem["sT"], t_gu)
                S.activation(silu_t[par][:], PS[:, bg * 512: bg * 512 + 1024],
                             AF.Silu, bias=cv(CV_ZERO),
                             scale=gs[:, i0 + ii:i0 + ii + 1]).then_inc(sem["sS"], 1)
                s_si = c.inc("sS")
                V.wait_ge(sem["sS"], s_si)
                V.scalar_tensor_tensor(hbuf[ii][:], PS[:, bu * 512: bu * 512 + 1024],
                                       us[:, i0 + ii:i0 + ii + 1], silu_t[par][:],
                                       ALU.mult, ALU.mult).then_inc(sem["sV"], 1)
                h_v[ii] = c.inc("sV")
            IH = IGRP // 2
            dw_w = [0] * (2 * NJ)
            d_v = [0] * NJ
            for j in range(NJ):
                par = j % 2
                bd = 4 * par
                for ih in range(2):
                    j2 = 2 * j + ih
                    slot = j2 % 2
                    G.wait_ge(sem["sT"], ring_dw[slot]) if ring_dw[slot] else None
                    col0 = (i0 + ih * IH) * 128
                    G.dma_start(out=dwb[slot][:],
                                 in_=dwS[j, :, col0:col0 + IH * 128]).then_inc(sem[f"dw{slot}"], 16)
                    dw_w[j2] = (f"dw{slot}", c.inc(f"dw{slot}", 16))
                if j == 0:
                    TE.wait_ge(sem["sV"], h_v[-1])
                if j >= 2:
                    TE.wait_ge(sem["sV"], d_v[j - 2])
                ins = None
                for ih in range(2):
                    slot = (2 * j + ih) % 2
                    TE.wait_ge(sem[dw_w[2 * j + ih][0]], dw_w[2 * j + ih][1])
                    for ii in range(IH):
                        gi = ih * IH + ii
                        for hh2 in range(2):
                            ins = TE.matmul(bk(bd + hh2), dwb[slot][:, ii * 128:(ii + 1) * 128],
                                            hbuf[gi][:, hh2 * 512: hh2 * 512 + 512],
                                            start=(gi == 0), stop=(gi == IGRP - 1))
                    ins.then_inc(sem["sT"], 1)
                    ring_dw[slot] = c.inc("sT")
                t_d = c.val("sT")
                V.wait_ge(sem["sT"], t_d)
                V.scalar_tensor_tensor(X[j][:], bk(bd, 1024), ds[:, j:j + 1], X[j][:],
                                       ALU.mult, ALU.add).then_inc(sem["sV"], 1)
                d_v[j] = c.inc("sV")
                st_marks[j] = d_v[j]
            prev_group_v = d_v[-1]
        so = 0
        for j in range(NJ):
            SY.wait_ge(sem["sV"], st_marks[j])
            SY.dma_start(out=yT[j * 128:(j + 1) * 128, f_idx * FSPAN:(f_idx + 1) * FSPAN],
                         in_=X[j][:]).then_inc(sem["sO"], 16)
            so = c.inc("sO", 16)
        last_store[f_idx] = so

    sched = [("A", 0), ("A", 1), ("F", 0), ("A", 2), ("A", 3), ("F", 1)][:nphases]
    for kind, idx in sched:
        if kind == "A":
            emit_A(idx)
        else:
            emit_F(idx)
        pass
    if debug:
        dbg_map = {
            "dXL1": (xln1, BF16), "dQ": (Q, BF16), "dK": (Kc, BF16),
            "dV": (Vc, BF16), "dAT": (attn, BF16), "dKT": (Ktail, BF16),
            "dVT": (Vtail, BF16), "dXL2": (xln2, BF16), "dHB": (hbuf, BF16),
            "dX": (X, F32),
        }
        SY.wait_ge(sem["sV"], c.val("sV"))
        SY.wait_ge(sem["sS"], c.val("sS"))
        SY.wait_ge(sem["sT"], c.val("sT"))
        for name in debug:
            tiles, dt_ = dbg_map[name]
            sh = [len(tiles) * 128, tiles[0].shape[1]]
            dout = nc.dram_tensor(name, sh, dt_, kind="ExternalOutput").ap()
            for j, tile in enumerate(tiles):
                SY.dma_start(out=dout[j * 128:(j + 1) * 128, :], in_=tile[:]).then_inc(sem["sO"], 16)
                c.inc("sO", 16)
    SY.wait_ge(sem["sO"], c.val("sO"))

    with nc.Block() as blk:
        blk.sync(lambda e: _replay(e, streams["sync"]))
        blk.gpsimd(lambda e: _replay(e, streams["gpsimd"]))
        blk.tensor(lambda e: _replay(e, streams["tensor"]))
        blk.vector(lambda e: _replay(e, streams["vector"]))
        blk.scalar(lambda e: _replay(e, streams["scalar"]))

    for cm in sem_cms.values():
        cm.__exit__(None, None, None)
    return nc


def _perm_qk():
    perm = []
    for p in range(H // 2):
        perm += [(2 * p) * DH + 2 * f for f in range(64)]
        perm += [(2 * p + 1) * DH + 2 * f for f in range(64)]
        perm += [(2 * p) * DH + 2 * f + 1 for f in range(64)]
        perm += [(2 * p + 1) * DH + 2 * f + 1 for f in range(64)]
    return np.array(perm)


def _strips(mT):
    K, M = mT.shape
    ni, nj = K // 128, M // 128
    s = mT.reshape(ni, 128, nj, 128).transpose(2, 1, 0, 3)
    return np.ascontiguousarray(s.reshape(nj, 128, K))


def _stage_inputs(L0_attn_x, L0_attn_Wq, L0_attn_Wk, L0_attn_Wv, L0_attn_Wo,
                  L0_ffn_gate_w, L0_ffn_up_w, L0_ffn_down_w,
                  L0_ffn_gate_s, L0_ffn_up_s, L0_ffn_down_s):
    x = np.asarray(L0_attn_x, dtype=np.float32)
    perm = _perm_qk()
    wqT = np.ascontiguousarray(np.asarray(L0_attn_Wq, np.float32).T[:, perm])
    wkT = np.ascontiguousarray(np.asarray(L0_attn_Wk, np.float32).T[:, perm])
    wvT = np.ascontiguousarray(np.asarray(L0_attn_Wv, np.float32).T)
    woT = np.ascontiguousarray(np.asarray(L0_attn_Wo, np.float32).T)
    gwT = np.asarray(L0_ffn_gate_w, np.float32).T
    uwT = np.asarray(L0_ffn_up_w, np.float32).T
    dwT = np.asarray(L0_ffn_down_w, np.float32).T

    wqS = _strips(wqT)
    wkS = _strips(wkT)
    woS = _strips(woT)
    gwS = _strips(gwT)
    uwS = _strips(uwT)
    dwS = _strips(dwT)

    gsv = np.ascontiguousarray(np.asarray(L0_ffn_gate_s, np.float32).reshape(NI, 128).T)
    usv = np.ascontiguousarray(np.asarray(L0_ffn_up_s, np.float32).reshape(NI, 128).T)
    dsv = np.ascontiguousarray(np.asarray(L0_ffn_down_s, np.float32).reshape(NJ, 128).T)

    trilv = np.where(np.arange(128)[None, :] < np.arange(128)[:, None], NEG, 0.0).astype(np.float32)
    onesSv = np.full((128, 128), 1.0 / C, np.float32)
    onesBv = np.ones((128, 128), bf16)
    onesB2v = np.full((128, 128), 1.0 / C, bf16)
    dgnegv = np.zeros((128, 128), np.float32)
    np.fill_diagonal(dgnegv, NEG)
    dgnegv = dgnegv.astype(bf16)
    ultriv = (np.arange(128)[None, :] < np.arange(128)[:, None]).astype(np.float32).astype(bf16)
    ultri2v = np.zeros((128, 256), np.float32)
    ultri2v[:, 0:128] = (np.arange(128)[None, :] < np.arange(128)[:, None])
    ultri2v = ultri2v.astype(bf16)
    cvecv = np.zeros((128, 8), np.float32)
    cvecv[:, CV_SHIFT] = -SHIFT
    cvecv[:, CV_SCALE] = SCALE
    cvecv[:, CV_EPS] = EPS
    cvecv[:, CV_NEGH] = -0.5
    cvecv[:, CV_ONE] = 1.0
    cvecv[:, CV_NEG1] = -1.0

    inv = 1.0 / 10000.0 ** (np.arange(0, DH, 2).astype(np.float32) / DH)
    in_maps = []
    for core in range(NCORES):
        b = core // 2
        half = core % 2
        t0 = half * TCORE
        xc = np.zeros((HALO + TCORE, C), np.float32)
        if half == 1:
            xc[0:HALO] = x[b, t0 - HALO:t0]
        xc[HALO:] = x[b, t0:t0 + TCORE]
        xTc = np.ascontiguousarray(xc.T)

        pos = np.arange(t0 - HALO, t0 + TCORE, dtype=np.float32)
        fr = np.outer(inv, pos)
        tc = np.cos(fr).astype(np.float32)
        tsn = np.sin(fr).astype(np.float32)
        trigCv = np.ascontiguousarray(np.concatenate([tc, tc], axis=0)).astype(bf16)
        trigSv = np.ascontiguousarray(np.concatenate([tsn, tsn], axis=0)).astype(bf16)
        bfv = np.full((128, 1), (-1e30 if half == 0 else -SHIFT), np.float32)

        in_maps.append({
            "xT": xTc, "wqS": wqS, "wkS": wkS, "wvT": wvT, "woS": woS,
            "gwS": gwS, "uwS": uwS, "dwS": dwS,
            "gsd": gsv, "usd": usv, "dsd": dsv,
            "trigCd": trigCv, "trigSd": trigSv,
            "trild": trilv, "onesSd": onesSv, "onesBd": onesBv, "onesB2d": onesB2v, "dgnegd": dgnegv, "ultrid": ultriv, "ultri2d": ultri2v,
            "cvecd": cvecv, "bfirstd": bfv,
        })
    return in_maps


def kernel(**inputs):
    in_maps = _stage_inputs(**inputs)
    nc = build_full()
    res = run_bass_kernel_spmd(nc, in_maps, core_ids=list(range(NCORES)))
    out = np.empty((B, T, C), np.float32)
    for core in range(NCORES):
        b = core // 2
        half = core % 2
        out[b, half * TCORE:(half + 1) * TCORE] = res.results[core]["yT"].T
    return out


# version 16 (final)
# speedup vs baseline: 1.0553x; 1.0553x over previous
import numpy as np
import ml_dtypes
import concourse.bass as bass
import concourse.mybir as mybir
from concourse.bass_utils import run_bass_kernel_spmd

F32 = mybir.dt.float32
BF16 = mybir.dt.bfloat16
F32R = mybir.dt.float32r
AF = mybir.ActivationFunctionType
ALU = mybir.AluOpType
bf16 = ml_dtypes.bfloat16

B, T, C = 4, 4096, 2048
H, DH = 16, 128
INTER = 5632
NCORES = 8
TCORE = 2048
HALO = 256
ASPAN = 512
FSPAN = 1024
NA = TCORE // ASPAN
NF = TCORE // FSPAN
NJ = C // 128
NI = INTER // 128
IGRP = 22
SHIFT = 14.0
SCALE = DH ** -0.5
NEG = -1.0e30
EPS = 1e-5

CV_SHIFT, CV_SCALE, CV_ZERO, CV_EPS, CV_NEGH, CV_ONE, CV_NEG1 = range(7)


class _Tok:
    def __init__(self, node):
        self.node = node

    def then_inc(self, s, n):
        self.node[3].append((s, n))
        return self


class _Eng:
    def __init__(self, stream):
        self._stream = stream

    def __getattr__(self, op):
        st = self._stream

        def call(*a, **k):
            node = (op, a, k, [])
            st.append(node)
            return _Tok(node)
        return call


def _replay(eng, stream):
    for op, a, k, incs in stream:
        ins = getattr(eng, op)(*a, **k)
        for s, n in incs:
            ins.then_inc(s, n)


class _Ctr:
    def __init__(self):
        self.v = {}

    def inc(self, name, k=1):
        self.v[name] = self.v.get(name, 0) + k
        return self.v[name]

    def val(self, name):
        return self.v.get(name, 0)


def build_full(nphases=6, debug=()):
    nc = bass.Bass("TRN2", target_bir_lowering=False, detect_race_conditions=False)

    dt = nc.dram_tensor
    xT = dt("xT", [C, HALO + TCORE], F32, kind="ExternalInput").ap()
    wqS = dt("wqS", [NJ, 128, C], F32, kind="ExternalInput").ap()
    wkS = dt("wkS", [NJ, 128, C], F32, kind="ExternalInput").ap()
    wvT = dt("wvT", [C, C], F32, kind="ExternalInput").ap()
    woS = dt("woS", [NJ, 128, C], F32, kind="ExternalInput").ap()
    gwS = dt("gwS", [NI, 128, C], F32, kind="ExternalInput").ap()
    uwS = dt("uwS", [NI, 128, C], F32, kind="ExternalInput").ap()
    dwS = dt("dwS", [NJ, 128, INTER], F32, kind="ExternalInput").ap()
    gs_d = dt("gsd", [128, NI], F32, kind="ExternalInput").ap()
    us_d = dt("usd", [128, NI], F32, kind="ExternalInput").ap()
    ds_d = dt("dsd", [128, NJ], F32, kind="ExternalInput").ap()
    trigC_d = dt("trigCd", [128, HALO + TCORE], F32, kind="ExternalInput").ap()
    trigS_d = dt("trigSd", [128, HALO + TCORE], F32, kind="ExternalInput").ap()
    tril_d = dt("trild", [128, 128], F32, kind="ExternalInput").ap()
    onesS_d = dt("onesSd", [128, 128], F32, kind="ExternalInput").ap()
    onesB_d = dt("onesBd", [128, 128], BF16, kind="ExternalInput").ap()
    onesB2_d = dt("onesB2d", [128, 128], BF16, kind="ExternalInput").ap()
    dgneg_d = dt("dgnegd", [128, 128], BF16, kind="ExternalInput").ap()
    ultri_d = dt("ultrid", [128, 128], BF16, kind="ExternalInput").ap()
    ultri2_d = dt("ultri2d", [128, 256], BF16, kind="ExternalInput").ap()
    cvec_d = dt("cvecd", [128, 8], F32, kind="ExternalInput").ap()
    bfirst_d = dt("bfirstd", [128, 1], F32, kind="ExternalInput").ap()
    yT = dt("yT", [C, TCORE], F32, kind="ExternalOutput").ap()

    al = nc.alloc_sbuf_tensor
    trigC = al("trigC", [128, HALO + TCORE], F32)
    trigS = al("trigS", [128, HALO + TCORE], F32)
    tril = al("tril", [128, 128], F32)
    onesS = al("onesS", [128, 128], F32)
    onesB = al("onesB", [128, 128], BF16)
    onesB2 = al("onesB2", [128, 128], BF16)
    dgneg = al("dgneg", [128, 128], BF16)
    ultri = al("ultri", [128, 128], BF16)
    ultri2 = al("ultri2", [128, 256], BF16)
    cvec = al("cvec", [128, 8], F32)
    bfirst = al("bfirst", [128, 1], F32)
    gs = al("gs", [128, NI], F32)
    us = al("us", [128, NI], F32)
    ds = al("ds", [128, NJ], F32)
    X = [al(f"X{j}", [128, FSPAN], F32) for j in range(NJ)]

    base = (nc.sbuf_base + 63) & ~63

    def at(name, shape, dtype, off):
        return nc.alloc_sbuf_tensor_at(name, shape, dtype, offset=base + off)

    Ktail = [at(f"Ktail{h}", [128, 256], BF16, h * 512) for h in range(H)]
    Vtail = [at(f"Vtail{i}", [128, 2048], BF16, 8192 + i * 4096) for i in range(2)]
    VOL = 16384
    xln1 = [at(f"xln1_{j}", [128, ASPAN], BF16, VOL + j * 1024) for j in range(NJ)]
    attn = [at(f"attn{j}", [128, ASPAN], BF16, VOL + j * 1024) for j in range(NJ)]
    Q = [at(f"Q{h}", [128, ASPAN], BF16, VOL + 16384 + h * 1024) for h in range(H)]
    Kc = [at(f"Kc{h}", [128, ASPAN], BF16, VOL + 32768 + h * 1024) for h in range(H)]
    Vc = [at(f"Vc{i}", [128, 2048], BF16, VOL + 49152 + i * 4096) for i in range(4)]
    wbuf = [at(f"wbuf{r}", [128, 2048], BF16, VOL + 65536 + r * 4096) for r in range(3)]
    sqkD = [at(f"sqkD{r}", [128, 4096], BF16, VOL + 77824 + r * 8192) for r in range(2)]
    sqk = [sqkD[r // 2][:, (r % 2) * 2048:(r % 2) * 2048 + 2048] for r in range(4)]
    WW = VOL + 94208
    xbb = [at(f"xbb{r}", [128, ASPAN], BF16, WW + r * 2048) for r in range(2)]
    xqb = [at(f"xqb{r}", [128, ASPAN], BF16, WW + r * 2048 + 1024) for r in range(2)]
    Msb = at("Msb", [128, ASPAN], F32, WW + 0)
    Rst = at("Rst", [128, ASPAN], F32, WW + 4096)
    lntmp = at("lntmp", [128, ASPAN], F32, WW + 6144)
    probsT = [at(f"probsT{r}", [128, 1024], BF16, WW + r * 2048) for r in range(2)]
    Rb = [at(f"Rb{r}", [128, 256], F32, WW + 4096 + r * 1024) for r in range(2)]
    ropet = [at(f"ropet{r}", [128, ASPAN], F32, WW + r * 2048) for r in range(4)]
    Xh = [at(f"Xh{j}", [128, 256], BF16, VOL + 16384 + j * 512) for j in range(NJ)]
    A_TOP = VOL + 102400
    xln2 = [at(f"xln2_{j}", [128, FSPAN], BF16, VOL + j * 2048) for j in range(NJ)]
    hbuf = [at(f"h{i}", [128, FSPAN], BF16, VOL + 32768 + i * 2048) for i in range(IGRP)]
    RstF = at("RstF", [128, FSPAN], F32, VOL + 32768)
    lntmpF = at("lntmpF", [128, FSPAN], F32, VOL + 36864)
    gwb = [at(f"gwb{r}", [128, 2048], BF16, VOL + 77824 + r * 4096) for r in range(2)]
    uwb = [at(f"uwb{r}", [128, 2048], BF16, VOL + 86016 + r * 4096) for r in range(2)]
    dwb = [at(f"dwb{r}", [128, (IGRP // 2) * 128], BF16, VOL + 94208 + r * 2816) for r in range(2)]
    xbbF = [at(f"xbbF{r}", [128, 512], BF16, VOL + 99840 + r * 2048) for r in range(2)]
    xqbF = [at(f"xqbF{r}", [128, 512], BF16, VOL + 99840 + r * 2048 + 1024) for r in range(2)]
    MsbF = at("MsbF", [128, FSPAN], F32, VOL + 99840)
    silu_t = [at(f"silu{r}", [128, FSPAN], BF16, VOL + 103936 + r * 2048) for r in range(2)]
    F_TOP = VOL + 108032
    assert base + max(A_TOP, F_TOP) <= 229376, (base, A_TOP, F_TOP, nc.sbuf_top)

    PS = nc.alloc_psum_tensor("PS", [128, 4096], F32)

    def bk(b, w=512, off=0):
        return PS[:, b * 512 + off: b * 512 + off + w]

    sem_names = (["sW", "sX", "sT", "sV", "sS", "sO", "sCN", "sXH",
                  "sXa", "sXb", "cWK", "cWV", "cWQ", "cWO", "cGU", "cDW"]
                 + [f"qk{r}" for r in range(2)] + [f"wb{r}" for r in range(3)]
                 + [f"gw{r}" for r in range(2)] + [f"uw{r}" for r in range(2)]
                 + [f"dw{r}" for r in range(2)])
    sem_cms = {n: nc.semaphore(n) for n in sem_names}
    sem = {n: cm.__enter__() for n, cm in sem_cms.items()}

    c = _Ctr()
    streams = {n: [] for n in ["sync", "gpsimd", "tensor", "vector", "scalar"]}
    SY = _Eng(streams["sync"])
    G = _Eng(streams["gpsimd"])
    TE = _Eng(streams["tensor"])
    V = _Eng(streams["vector"])
    S = _Eng(streams["scalar"])

    def cv(i):
        return cvec[:, i:i + 1]

    ring_qk = [0, 0]
    ring_wv = [0, 0, 0]
    ring_gw = [0, 0]
    ring_uw = [0, 0]
    ring_dw = [0, 0]
    xsq_mark = [0, 0]
    xsqF_mark = [0, 0]
    cmark = {}

    for dst, src in [(trigC, trigC_d), (trigS, trigS_d), (tril, tril_d),
                     (onesS, onesS_d), (onesB, onesB_d), (onesB2, onesB2_d), (dgneg, dgneg_d), (ultri, ultri_d), (ultri2, ultri2_d), (cvec, cvec_d),
                     (bfirst, bfirst_d), (gs, gs_d), (us, us_d), (ds, ds_d)]:
        SY.dma_start(out=dst[:], in_=src).then_inc(sem["sCN"], 16)
        c.inc("sCN", 16)
    CONSTS_X = c.val("sCN")
    for E in (TE, V, S):
        E.wait_ge(sem["sCN"], CONSTS_X)

    def barrier():
        return {"sT": c.val("sT"), "sV": c.val("sV"), "sS": c.val("sS"),
                "sW": c.val("sW"), "sX": c.val("sX"), "sO": c.val("sO")}

    def wait_barrier(E, bm, *names):
        for n in names:
            if bm[n] > 0:
                E.wait_ge(sem[n], bm[n])

    def emit_ln(span, x_ap, xb_tiles, xq_tiles, xq_marks, xout, Rtile, tmptile, msb, pre_wait):
        nh = (span + 511) // 512
        b_m, b_sq, b_var = 0, nh, 2 * nh
        sq_t = []
        sq_s = {}
        for j in range(NJ):
            if j == 0:
                for sname, sval in pre_wait.get("V", []):
                    V.wait_ge(sem[sname], sval)
            for hh in range(nh):
                k = j * nh + hh
                slot = k % 2
                w = min(512, span - hh * 512)
                if xq_marks[slot] > 0:
                    V.wait_ge(sem["sT"], xq_marks[slot])
                V.tensor_copy(xb_tiles[slot][:, 0:w],
                              x_ap(j)[:, hh * 512: hh * 512 + w]).then_inc(sem["sV"], 1)
                cast_m = c.inc("sV")
                if k == 0:
                    for sname, sval in pre_wait.get("S", []):
                        S.wait_ge(sem[sname], sval)
                S.wait_ge(sem["sV"], cast_m)
                S.square(xq_tiles[slot][:, 0:w],
                         xb_tiles[slot][:, 0:w]).then_inc(sem["sS"], 1)
                sq_s[k] = c.inc("sS")
            if j == 0:
                for sname, sval in pre_wait.get("T", []):
                    TE.wait_ge(sem[sname], sval)
            for hh in range(nh):
                k = j * nh + hh
                slot = k % 2
                w = min(512, span - hh * 512)
                TE.wait_ge(sem["sS"], sq_s[k])
                TE.matmul(bk(b_m + hh, w), onesB2[:], xb_tiles[slot][:, 0:w],
                          start=(j == 0), stop=(j == NJ - 1))
                TE.matmul(bk(b_sq + hh, w), onesB2[:], xq_tiles[slot][:, 0:w],
                          start=(j == 0), stop=(j == NJ - 1)).then_inc(sem["sT"], 1)
                xq_marks[slot] = c.inc("sT")
            sq_t.append(xq_marks[(j * nh + nh - 1) % 2])
        V.wait_ge(sem["sT"], sq_t[-1])
        V.tensor_copy(msb[:, 0:span], PS[:, b_m * 512: b_m * 512 + span])
        V.scalar_tensor_tensor(tmptile[:, 0:span],
                               PS[:, b_m * 512: b_m * 512 + span], cv(CV_NEG1),
                               msb[:, 0:span], ALU.mult, ALU.mult)
        V.tensor_add(PS[:, b_var * 512: b_var * 512 + span],
                     PS[:, b_sq * 512: b_sq * 512 + span],
                     tmptile[:, 0:span]).then_inc(sem["sV"], 1)
        var_m = c.inc("sV")
        S.wait_ge(sem["sV"], var_m)
        S.activation(tmptile[:, 0:span], PS[:, b_var * 512: b_var * 512 + span],
                     AF.Ln, bias=cv(CV_EPS), scale=cv(CV_ONE))
        S.activation(Rtile[:, 0:span], tmptile[:, 0:span], AF.Exp,
                     bias=cv(CV_ZERO), scale=cv(CV_NEGH)).then_inc(sem["sS"], 1)
        r_m = c.inc("sS")
        V.wait_ge(sem["sS"], r_m)
        for j in range(NJ):
            V.tensor_sub(tmptile[:, 0:span], x_ap(j), msb[:, 0:span])
            ins = V.tensor_mul(xout(j), tmptile[:, 0:span], Rtile[:, 0:span])
        ins.then_inc(sem["sV"], 1)
        return c.inc("sV"), b_m

    def emit_qk_proj(which, strips_pair_d, scratch, dst, trig0, span, xsrc, ln_mark, bm):
        pair_v = [0] * (H // 2)
        str_t = [0] * NJ
        str_w = [0] * NJ
        for j in range(NJ):
            p = j // 2
            if j % 2 == 0:
                dslot = p % 2
                sl = f"qk{dslot}"
                G.wait_ge(sem["sT"], ring_qk[dslot]) if ring_qk[dslot] else None
                G.dma_start(out=sqkD[dslot][:].rearrange("p (s m) -> p s m", s=2),
                            in_=strips_pair_d[p]).then_inc(sem[sl], 16)
                str_w[j] = str_w[min(j + 1, NJ - 1)] = (sl, c.inc(sl, 16))
            slot = (p % 2) * 2 + (j % 2)
            pb = p % 2
            b = (j % 2) + 2 * pb
            if j == 0:
                TE.wait_ge(sem["sV"], ln_mark)
                wait_barrier(TE, bm, "sV")
            TE.wait_ge(sem[str_w[j][0]], str_w[j][1])
            if p >= 2 and j % 2 == 0:
                TE.wait_ge(sem["sV"], pair_v[p - 2])
            ins = None
            for i in range(NJ):
                ins = TE.matmul(bk(b, span), sqk[slot][:, i * 128:(i + 1) * 128],
                                xsrc[i][:, 0:span], start=(i == 0), stop=(i == NJ - 1))
            ins.then_inc(sem["sT"], 1)
            str_t[j] = c.inc("sT")
            if j % 2 == 1:
                ring_qk[p % 2] = str_t[j]
            if j % 2 == 1:
                V.wait_ge(sem["sT"], str_t[j])
                bR, bI = 2 * pb, 2 * pb + 1
                tc_ = trigC[:, trig0:trig0 + span]
                ts_ = trigS[:, trig0:trig0 + span]
                d0, d1 = dst[2 * p], dst[2 * p + 1]
                V.tensor_mul(ropet[0][:, 0:span], bk(bR, span), tc_)
                V.tensor_mul(ropet[1][:, 0:span], bk(bI, span), ts_)
                V.tensor_mul(ropet[2][:, 0:span], bk(bR, span), ts_)
                V.tensor_mul(ropet[3][:, 0:span], bk(bI, span), tc_)
                V.tensor_sub(d0[0:64, 0:span], ropet[0][0:64, 0:span], ropet[1][0:64, 0:span])
                V.tensor_sub(d1[0:64, 0:span], ropet[0][64:128, 0:span], ropet[1][64:128, 0:span])
                V.tensor_add(d0[64:128, 0:span], ropet[2][0:64, 0:span], ropet[3][0:64, 0:span])
                V.tensor_add(d1[64:128, 0:span], ropet[2][64:128, 0:span],
                             ropet[3][64:128, 0:span]).then_inc(sem["sV"], 1)
                pair_v[p] = c.inc("sV")
        return pair_v[-1]

    def emit_v_proj(ntj, xsrc, dstV, prev_v_mark):
        ev_last = prev_v_mark
        for dhalf in range(2):
            w_m = [0] * (NJ // 2)
            for ip in range(NJ // 2):
                slot = ip % 3
                sl = f"wb{slot}"
                G.wait_ge(sem["sT"], ring_wv[slot]) if ring_wv[slot] else None
                srcv = wvT[ip * 256: ip * 256 + 256,
                           dhalf * 1024: dhalf * 1024 + 1024]
                srcv = srcv.rearrange("(s p) m -> p s m", p=128)
                G.dma_start(out=wbuf[slot][:].rearrange("p (s m) -> p s m", s=2),
                            in_=srcv).then_inc(sem[sl], 16)
                w_m[ip] = (sl, c.inc(sl, 16))
                for ii in range(2):
                    i = 2 * ip + ii
                    if i == 0:
                        if isinstance(ev_last, tuple):
                            TE.wait_ge(sem[ev_last[0]], ev_last[1])
                        else:
                            TE.wait_ge(sem["sV"], ev_last)
                    if ii == 0:
                        TE.wait_ge(sem[w_m[ip][0]], w_m[ip][1])
                    ins = None
                    for tj in range(ntj):
                        for d2 in range(2):
                            ins = TE.matmul(
                                PS[:, tj * 1024 + d2 * 512: tj * 1024 + d2 * 512 + 512],
                                xsrc[i][:, tj * 128: tj * 128 + 128],
                                wbuf[slot][:, ii * 1024 + d2 * 512: ii * 1024 + d2 * 512 + 512],
                                start=(i == 0), stop=(i == NJ - 1))
                    ins.then_inc(sem["sT"], 1)
                    ring_wv[slot] = c.inc("sT")
            S.wait_ge(sem["sT"], ring_wv[(NJ // 2 - 1) % 3])
            ins = None
            for tj in range(ntj):
                ins = S.copy(dstV[tj][:, dhalf * 1024: dhalf * 1024 + 1024],
                             PS[:, tj * 1024: (tj + 1) * 1024])
            ins.then_inc(sem["sS"], 1)
            ev_last = ("sS", c.inc("sS"))
        return ev_last

    bm = barrier()
    xh_w = 0
    for j in range(NJ):
        G.dma_start(out=Xh[j][:], in_=xT[j * 128:(j + 1) * 128, 0:HALO]).then_inc(sem["sXH"], 16)
        xh_w = c.inc("sXH", 16)
    hsq_v = []
    hsq_t = []
    for j in range(NJ):
        if j == 0:
            V.wait_ge(sem["sXH"], xh_w)
        if j >= 2:
            V.wait_ge(sem["sT"], hsq_t[j - 2])
        V.tensor_mul(xqb[j % 2][:, 0:HALO], Xh[j][:], Xh[j][:]).then_inc(sem["sV"], 1)
        hsq_v.append(c.inc("sV"))
        if j == 0:
            TE.wait_ge(sem["sXH"], xh_w)
        TE.matmul(bk(0, HALO), onesB2[:], Xh[j][:], start=(j == 0), stop=(j == NJ - 1))
        TE.wait_ge(sem["sV"], hsq_v[j])
        TE.matmul(bk(1, HALO), onesB2[:], xqb[j % 2][:, 0:HALO],
                  start=(j == 0), stop=(j == NJ - 1)).then_inc(sem["sT"], 1)
        hsq_t.append(c.inc("sT"))
        xsq_mark[j % 2] = hsq_t[-1]
    V.wait_ge(sem["sT"], hsq_t[-1])
    V.tensor_copy(Msb[:, 0:HALO], bk(0, HALO))
    V.scalar_tensor_tensor(lntmp[:, 0:HALO], bk(0, HALO), cv(CV_NEG1),
                           Msb[:, 0:HALO], ALU.mult, ALU.mult)
    V.tensor_add(bk(2, HALO), bk(1, HALO), lntmp[:, 0:HALO]).then_inc(sem["sV"], 1)
    h_var = c.inc("sV")
    S.wait_ge(sem["sV"], h_var)
    S.activation(lntmp[:, 0:HALO], bk(2, HALO), AF.Ln, bias=cv(CV_EPS), scale=cv(CV_ONE))
    S.activation(Rst[:, 0:HALO], lntmp[:, 0:HALO], AF.Exp,
                 bias=cv(CV_ZERO), scale=cv(CV_NEGH)).then_inc(sem["sS"], 1)
    h_r = c.inc("sS")
    V.wait_ge(sem["sS"], h_r)
    for j in range(NJ):
        V.tensor_sub(lntmp[:, 0:HALO], Xh[j][:], Msb[:, 0:HALO])
        ins = V.tensor_mul(xln1[j][:, 0:HALO], lntmp[:, 0:HALO], Rst[:, 0:HALO])
    ins.then_inc(sem["sV"], 1)
    h_ln = c.inc("sV")

    wk_pairs = [wkS[2 * p:2 * p + 2, :, :].rearrange("s p m -> p s m") for p in range(NJ // 2)]
    klast = emit_qk_proj("wk", wk_pairs, None, Ktail, 0, HALO, [xln1[j] for j in range(NJ)], h_ln, bm)
    vlast = emit_v_proj(2, [xln1[j] for j in range(NJ)], Vtail, klast)

    last_store = {}

    def bump_rings(bm):
        for r in (ring_qk, ring_wv, ring_gw, ring_uw, ring_dw, xsq_mark, xsqF_mark):
            for idx in range(len(r)):
                r[idx] = max(r[idx], bm["sT"])

    def emit_A(t_idx):
        nonlocal vlast
        bm = barrier()
        if t_idx == 2:
            bump_rings(bm)
        for r in (xsq_mark, xsqF_mark):
            for idx in range(len(r)):
                r[idx] = max(r[idx], bm["sT"])
        acol0 = ASPAN * t_idx
        xcol0 = ASPAN * (t_idx % 2)
        tg0 = HALO + acol0

        f_prev = t_idx // 2 - 1
        if f_prev in last_store:
            SY.wait_ge(sem["sO"], last_store[f_prev])
        xs = "sXa" if t_idx % 2 == 0 else "sXb"
        xw = 0
        for j in range(NJ):
            SY.dma_start(out=X[j][:, xcol0:xcol0 + ASPAN],
                         in_=xT[j * 128:(j + 1) * 128, tg0:tg0 + ASPAN]).then_inc(sem[xs], 16)
            xw = c.inc(xs, 16)

        pre = {"V": [(xs, xw)], "S": [(xs, xw)],
               "T": [(xs, xw), ("sV", bm["sV"]), ("sS", bm["sS"])]}
        ln_mark, _ = emit_ln(ASPAN, lambda j: X[j][:, xcol0:xcol0 + ASPAN],
                             xbb, xqb, xsq_mark, lambda j: xln1[j][:],
                             Rst, lntmp, Msb, pre)

        wq_pairs = [wqS[2 * p:2 * p + 2, :, :].rearrange("s p m -> p s m") for p in range(NJ // 2)]
        wk_pairs = [wkS[2 * p:2 * p + 2, :, :].rearrange("s p m -> p s m") for p in range(NJ // 2)]
        emit_qk_proj("wq", wq_pairs, None, Q, tg0, ASPAN, xln1, ln_mark, bm)
        kp = emit_qk_proj("wk", wk_pairs, None, Kc, tg0, ASPAN, xln1, ln_mark, bm)
        vlast = emit_v_proj(4, xln1, Vc, kp)
        V.memset(probsT[0][:, 768:896], 0.0)
        V.memset(probsT[1][:, 768:896], 0.0).then_inc(sem["sV"], 1)
        pad_m = c.inc("sV")

        def kblk(ch):
            if ch == 0:
                return [(Ktail, 0), (Ktail, 128), (Kc, 0), (Kc, 128)]
            return [(Kc, 0), (Kc, 128), (Kc, 256), (Kc, 384)]

        def vblk(ch):
            if ch == 0:
                return [Vtail[0], Vtail[1], Vc[0], Vc[1]]
            return [Vc[0], Vc[1], Vc[2], Vc[3]]

        NCH = 2 * H
        av_ev = [0] * NCH
        for ch in range(2):
            kb = kblk(ch)
            vb = vblk(ch)
            for h in range(H):
                n = ch * H + h
                par = n % 2
                sc_b, sm_b, av_b = 4 * par, 4 * par + 2, 4 * par + 3
                if n == 0:
                    TE.wait_ge(sem[vlast[0]], vlast[1])
                    TE.wait_ge(sem["sV"], pad_m)
                if n >= 2:
                    TE.wait_ge(sem["sV"], av_ev[n - 2])
                qmv = Q[h][:, ch * 256: ch * 256 + 256]
                for b in range(3):
                    tile, col = kb[b]
                    TE.matmul(PS[:, sc_b * 512 + b * 256: sc_b * 512 + b * 256 + 256],
                              tile[h][:, col:col + 128], qmv, start=True, stop=(b != 2))
                TE.matmul(PS[:, sc_b * 512 + 512: sc_b * 512 + 768],
                          dgneg[:], ultri2[:], start=False, stop=True)
                tile, col = kb[3]
                TE.matmul(PS[:, sc_b * 512 + 896: sc_b * 512 + 1024],
                          tile[h][:, col:col + 128],
                          Q[h][:, ch * 256 + 128: ch * 256 + 256],
                          start=True, stop=False)
                TE.matmul(PS[:, sc_b * 512 + 896: sc_b * 512 + 1024],
                          dgneg[:], ultri[:], start=False, stop=True).then_inc(sem["sT"], 1)
                t_sc = c.inc("sT")
                S.wait_ge(sem["sT"], t_sc)
                pt = probsT[par]
                if t_idx == 0 and ch == 0:
                    S.activation(pt[:, 0:512], PS[:, sc_b * 512: sc_b * 512 + 512],
                                 AF.Exp, bias=bfirst[:], scale=cv(CV_SCALE))
                    S.activation(pt[:, 512:768], PS[:, sc_b * 512 + 512: sc_b * 512 + 768],
                                 AF.Exp, bias=cv(CV_SHIFT), scale=cv(CV_SCALE))
                else:
                    S.activation(pt[:, 0:768], PS[:, sc_b * 512: sc_b * 512 + 768],
                                 AF.Exp, bias=cv(CV_SHIFT), scale=cv(CV_SCALE))
                S.activation(pt[:, 896:1024], PS[:, sc_b * 512 + 896: sc_b * 512 + 1024],
                             AF.Exp, bias=cv(CV_SHIFT), scale=cv(CV_SCALE)).then_inc(sem["sS"], 1)
                s_exp = c.inc("sS")
                TE.wait_ge(sem["sS"], s_exp)
                for b in range(3):
                    TE.matmul(PS[:, sm_b * 512: sm_b * 512 + 256], onesB[:],
                              pt[:, b * 256:(b + 1) * 256], start=(b == 0), stop=False)
                TE.matmul(PS[:, sm_b * 512: sm_b * 512 + 256], onesB[:],
                          pt[:, 768:1024], start=False, stop=True)
                for b in range(3):
                    TE.matmul(PS[:, av_b * 512: av_b * 512 + 256],
                              vb[b][:, h * 128:(h + 1) * 128],
                              pt[:, b * 256:(b + 1) * 256], start=(b == 0), stop=False)
                TE.matmul(PS[:, av_b * 512: av_b * 512 + 256],
                          vb[3][:, h * 128:(h + 1) * 128],
                          pt[:, 768:1024], start=False, stop=True).then_inc(sem["sT"], 1)
                t_av = c.inc("sT")
                V.wait_ge(sem["sT"], t_av)
                V.reciprocal(Rb[par][:], PS[:, sm_b * 512: sm_b * 512 + 256])
                V.tensor_mul(attn[h][:, ch * 256: ch * 256 + 256],
                             PS[:, av_b * 512: av_b * 512 + 256],
                             Rb[par][:]).then_inc(sem["sV"], 1)
                av_ev[n] = c.inc("sV")

        wo_strips = [woS[j, :, :] for j in range(NJ)]
        o_w = [0] * NJ
        o_v = [0] * NJ
        for j in range(NJ):
            slot = j % 3
            sl = f"wb{slot}"
            G.wait_ge(sem["sT"], ring_wv[slot]) if ring_wv[slot] else None
            G.dma_start(out=wbuf[slot][:], in_=wo_strips[j]).then_inc(sem[sl], 16)
            o_w[j] = (sl, c.inc(sl, 16))
            b = j % 3
            if j == 0:
                TE.wait_ge(sem["sV"], av_ev[-1])
            TE.wait_ge(sem[o_w[j][0]], o_w[j][1])
            if j >= 3:
                TE.wait_ge(sem["sV"], o_v[j - 3])
            ins = None
            for i in range(NJ):
                ins = TE.matmul(bk(b), wbuf[slot][:, i * 128:(i + 1) * 128],
                                attn[i][:], start=(i == 0), stop=(i == NJ - 1))
            ins.then_inc(sem["sT"], 1)
            t_o = c.inc("sT")
            ring_wv[slot] = t_o
            V.wait_ge(sem["sT"], t_o)
            V.scalar_tensor_tensor(X[j][:, xcol0:xcol0 + ASPAN], bk(b), cv(CV_ONE),
                                   X[j][:, xcol0:xcol0 + ASPAN],
                                   ALU.mult, ALU.add).then_inc(sem["sV"], 1)
            o_v[j] = c.inc("sV")

        S.wait_ge(sem["sT"], c.val("sT"))
        for hh in range(H):
            S.copy(Ktail[hh][:], Kc[hh][:, 256:512])
        for i in range(2):
            ins = S.copy(Vtail[i][:], Vc[2 + i][:])
        ins.then_inc(sem["sS"], 1)
        c.inc("sS")

    def emit_F(f_idx):
        bm = barrier()
        bump_rings(bm)
        pre = {"V": [("sV", bm["sV"])], "S": [("sS", bm["sS"])],
               "T": [("sV", bm["sV"]), ("sS", bm["sS"])]}
        ln_mark, _ = emit_ln(FSPAN, lambda j: X[j][:], xbbF, xqbF, xsqF_mark,
                             lambda j: xln2[j][:], RstF, lntmpF, MsbF, pre)

        prev_group_v = 0
        st_marks = [0] * NJ
        for grp in range(2):
            i0 = grp * IGRP
            gu_w = [0] * IGRP
            h_v = [0] * IGRP
            for ii in range(IGRP):
                slot = ii % 2
                if grp == 0 and ii < 2:
                    wait_barrier(G, bm, "sT")
                G.wait_ge(sem["sT"], ring_gw[slot]) if ring_gw[slot] else None
                G.dma_start(out=gwb[slot][:], in_=gwS[i0 + ii, :, :]).then_inc(sem[f"gw{slot}"], 16)
                gm = (f"gw{slot}", c.inc(f"gw{slot}", 16))
                G.wait_ge(sem["sT"], ring_uw[slot]) if ring_uw[slot] else None
                G.dma_start(out=uwb[slot][:], in_=uwS[i0 + ii, :, :]).then_inc(sem[f"uw{slot}"], 16)
                gu_w[ii] = (f"uw{slot}", c.inc(f"uw{slot}", 16), gm)
                par = ii % 2
                bg, bu = 4 * par, 4 * par + 2
                if ii == 0:
                    TE.wait_ge(sem["sV"], ln_mark if grp == 0 else prev_group_v)
                TE.wait_ge(sem[gu_w[ii][0]], gu_w[ii][1])
                TE.wait_ge(sem[gu_w[ii][2][0]], gu_w[ii][2][1])
                if ii >= 2:
                    TE.wait_ge(sem["sV"], h_v[ii - 2])
                for i in range(NJ):
                    for hh2 in range(2):
                        TE.matmul(bk(bg + hh2), gwb[slot][:, i * 128:(i + 1) * 128],
                                  xln2[i][:, hh2 * 512: hh2 * 512 + 512],
                                  start=(i == 0), stop=(i == NJ - 1))
                ins = None
                for i in range(NJ):
                    for hh2 in range(2):
                        ins = TE.matmul(bk(bu + hh2), uwb[slot][:, i * 128:(i + 1) * 128],
                                        xln2[i][:, hh2 * 512: hh2 * 512 + 512],
                                        start=(i == 0), stop=(i == NJ - 1))
                ins.then_inc(sem["sT"], 1)
                t_gu = c.inc("sT")
                ring_gw[slot] = t_gu
                ring_uw[slot] = t_gu
                S.wait_ge(sem["sT"], t_gu)
                S.activation(silu_t[par][:], PS[:, bg * 512: bg * 512 + 1024],
                             AF.Silu, bias=cv(CV_ZERO),
                             scale=gs[:, i0 + ii:i0 + ii + 1]).then_inc(sem["sS"], 1)
                s_si = c.inc("sS")
                V.wait_ge(sem["sS"], s_si)
                V.scalar_tensor_tensor(hbuf[ii][:], PS[:, bu * 512: bu * 512 + 1024],
                                       us[:, i0 + ii:i0 + ii + 1], silu_t[par][:],
                                       ALU.mult, ALU.mult).then_inc(sem["sV"], 1)
                h_v[ii] = c.inc("sV")
            IH = IGRP // 2
            dw_w = [0] * (2 * NJ)
            d_v = [0] * NJ
            for j in range(NJ):
                par = j % 2
                bd = 4 * par
                for ih in range(2):
                    j2 = 2 * j + ih
                    slot = j2 % 2
                    G.wait_ge(sem["sT"], ring_dw[slot]) if ring_dw[slot] else None
                    col0 = (i0 + ih * IH) * 128
                    G.dma_start(out=dwb[slot][:],
                                 in_=dwS[j, :, col0:col0 + IH * 128]).then_inc(sem[f"dw{slot}"], 16)
                    dw_w[j2] = (f"dw{slot}", c.inc(f"dw{slot}", 16))
                if j == 0:
                    TE.wait_ge(sem["sV"], h_v[-1])
                if j >= 2:
                    TE.wait_ge(sem["sV"], d_v[j - 2])
                ins = None
                for ih in range(2):
                    slot = (2 * j + ih) % 2
                    TE.wait_ge(sem[dw_w[2 * j + ih][0]], dw_w[2 * j + ih][1])
                    for ii in range(IH):
                        gi = ih * IH + ii
                        for hh2 in range(2):
                            ins = TE.matmul(bk(bd + hh2), dwb[slot][:, ii * 128:(ii + 1) * 128],
                                            hbuf[gi][:, hh2 * 512: hh2 * 512 + 512],
                                            start=(gi == 0), stop=(gi == IGRP - 1))
                    ins.then_inc(sem["sT"], 1)
                    ring_dw[slot] = c.inc("sT")
                t_d = c.val("sT")
                V.wait_ge(sem["sT"], t_d)
                V.scalar_tensor_tensor(X[j][:], bk(bd, 1024), ds[:, j:j + 1], X[j][:],
                                       ALU.mult, ALU.add).then_inc(sem["sV"], 1)
                d_v[j] = c.inc("sV")
                st_marks[j] = d_v[j]
            prev_group_v = d_v[-1]
        so = 0
        for j in range(NJ):
            SY.wait_ge(sem["sV"], st_marks[j])
            SY.dma_start(out=yT[j * 128:(j + 1) * 128, f_idx * FSPAN:(f_idx + 1) * FSPAN],
                         in_=X[j][:]).then_inc(sem["sO"], 16)
            so = c.inc("sO", 16)
        last_store[f_idx] = so

    sched = [("A", 0), ("A", 1), ("F", 0), ("A", 2), ("A", 3), ("F", 1)][:nphases]
    for kind, idx in sched:
        if kind == "A":
            emit_A(idx)
        else:
            emit_F(idx)
        pass
    if debug:
        dbg_map = {
            "dXL1": (xln1, BF16), "dQ": (Q, BF16), "dK": (Kc, BF16),
            "dV": (Vc, BF16), "dAT": (attn, BF16), "dKT": (Ktail, BF16),
            "dVT": (Vtail, BF16), "dXL2": (xln2, BF16), "dHB": (hbuf, BF16),
            "dX": (X, F32),
        }
        SY.wait_ge(sem["sV"], c.val("sV"))
        SY.wait_ge(sem["sS"], c.val("sS"))
        SY.wait_ge(sem["sT"], c.val("sT"))
        for name in debug:
            tiles, dt_ = dbg_map[name]
            sh = [len(tiles) * 128, tiles[0].shape[1]]
            dout = nc.dram_tensor(name, sh, dt_, kind="ExternalOutput").ap()
            for j, tile in enumerate(tiles):
                SY.dma_start(out=dout[j * 128:(j + 1) * 128, :], in_=tile[:]).then_inc(sem["sO"], 16)
                c.inc("sO", 16)
    SY.wait_ge(sem["sO"], c.val("sO"))

    with nc.Block() as blk:
        blk.sync(lambda e: _replay(e, streams["sync"]))
        blk.gpsimd(lambda e: _replay(e, streams["gpsimd"]))
        blk.tensor(lambda e: _replay(e, streams["tensor"]))
        blk.vector(lambda e: _replay(e, streams["vector"]))
        blk.scalar(lambda e: _replay(e, streams["scalar"]))

    for cm in sem_cms.values():
        cm.__exit__(None, None, None)
    return nc


def _perm_qk():
    perm = []
    for p in range(H // 2):
        perm += [(2 * p) * DH + 2 * f for f in range(64)]
        perm += [(2 * p + 1) * DH + 2 * f for f in range(64)]
        perm += [(2 * p) * DH + 2 * f + 1 for f in range(64)]
        perm += [(2 * p + 1) * DH + 2 * f + 1 for f in range(64)]
    return np.array(perm)


def _strips(mT):
    K, M = mT.shape
    ni, nj = K // 128, M // 128
    s = mT.reshape(ni, 128, nj, 128).transpose(2, 1, 0, 3)
    return np.ascontiguousarray(s.reshape(nj, 128, K))


def _stage_inputs(L0_attn_x, L0_attn_Wq, L0_attn_Wk, L0_attn_Wv, L0_attn_Wo,
                  L0_ffn_gate_w, L0_ffn_up_w, L0_ffn_down_w,
                  L0_ffn_gate_s, L0_ffn_up_s, L0_ffn_down_s):
    x = np.asarray(L0_attn_x, dtype=np.float32)
    perm = _perm_qk()
    wqT = np.ascontiguousarray(np.asarray(L0_attn_Wq, np.float32).T[:, perm])
    wkT = np.ascontiguousarray(np.asarray(L0_attn_Wk, np.float32).T[:, perm])
    wvT = np.ascontiguousarray(np.asarray(L0_attn_Wv, np.float32).T)
    woT = np.ascontiguousarray(np.asarray(L0_attn_Wo, np.float32).T)
    gwT = np.asarray(L0_ffn_gate_w, np.float32).T
    uwT = np.asarray(L0_ffn_up_w, np.float32).T
    dwT = np.asarray(L0_ffn_down_w, np.float32).T

    wqS = _strips(wqT)
    wkS = _strips(wkT)
    woS = _strips(woT)
    gwS = _strips(gwT)
    uwS = _strips(uwT)
    dwS = _strips(dwT)

    gsv = np.ascontiguousarray(np.asarray(L0_ffn_gate_s, np.float32).reshape(NI, 128).T)
    usv = np.ascontiguousarray(np.asarray(L0_ffn_up_s, np.float32).reshape(NI, 128).T)
    dsv = np.ascontiguousarray(np.asarray(L0_ffn_down_s, np.float32).reshape(NJ, 128).T)

    trilv = np.where(np.arange(128)[None, :] < np.arange(128)[:, None], NEG, 0.0).astype(np.float32)
    onesSv = np.full((128, 128), 1.0 / C, np.float32)
    onesBv = np.ones((128, 128), bf16)
    onesB2v = np.full((128, 128), 1.0 / C, bf16)
    dgnegv = np.zeros((128, 128), np.float32)
    np.fill_diagonal(dgnegv, NEG)
    dgnegv = dgnegv.astype(bf16)
    ultriv = (np.arange(128)[None, :] < np.arange(128)[:, None]).astype(np.float32).astype(bf16)
    ultri2v = np.zeros((128, 256), np.float32)
    ultri2v[:, 0:128] = (np.arange(128)[None, :] < np.arange(128)[:, None])
    ultri2v = ultri2v.astype(bf16)
    cvecv = np.zeros((128, 8), np.float32)
    cvecv[:, CV_SHIFT] = -SHIFT
    cvecv[:, CV_SCALE] = SCALE
    cvecv[:, CV_EPS] = EPS
    cvecv[:, CV_NEGH] = -0.5
    cvecv[:, CV_ONE] = 1.0
    cvecv[:, CV_NEG1] = -1.0

    inv = 1.0 / 10000.0 ** (np.arange(0, DH, 2).astype(np.float32) / DH)
    in_maps = []
    for core in range(NCORES):
        b = core // 2
        half = core % 2
        t0 = half * TCORE
        xc = np.zeros((HALO + TCORE, C), np.float32)
        if half == 1:
            xc[0:HALO] = x[b, t0 - HALO:t0]
        xc[HALO:] = x[b, t0:t0 + TCORE]
        xTc = np.ascontiguousarray(xc.T)

        pos = np.arange(t0 - HALO, t0 + TCORE, dtype=np.float32)
        fr = np.outer(inv, pos)
        tc = np.cos(fr).astype(np.float32)
        tsn = np.sin(fr).astype(np.float32)
        trigCv = np.ascontiguousarray(np.concatenate([tc, tc], axis=0))
        trigSv = np.ascontiguousarray(np.concatenate([tsn, tsn], axis=0))
        bfv = np.full((128, 1), (-1e30 if half == 0 else -SHIFT), np.float32)

        in_maps.append({
            "xT": xTc, "wqS": wqS, "wkS": wkS, "wvT": wvT, "woS": woS,
            "gwS": gwS, "uwS": uwS, "dwS": dwS,
            "gsd": gsv, "usd": usv, "dsd": dsv,
            "trigCd": trigCv, "trigSd": trigSv,
            "trild": trilv, "onesSd": onesSv, "onesBd": onesBv, "onesB2d": onesB2v, "dgnegd": dgnegv, "ultrid": ultriv, "ultri2d": ultri2v,
            "cvecd": cvecv, "bfirstd": bfv,
        })
    return in_maps


def kernel(**inputs):
    in_maps = _stage_inputs(**inputs)
    nc = build_full()
    res = run_bass_kernel_spmd(nc, in_maps, core_ids=list(range(NCORES)))
    out = np.empty((B, T, C), np.float32)
    for core in range(NCORES):
        b = core // 2
        half = core % 2
        out[b, half * TCORE:(half + 1) * TCORE] = res.results[core]["yT"].T
    return out
